# version 02
# baseline (speedup 1.0000x reference)
import math
import numpy as np
import ml_dtypes

import concourse.bass as bass
import concourse.mybir as mybir
from concourse.bass_utils import run_bass_kernel_spmd

F32 = mybir.dt.float32
BF16 = mybir.dt.bfloat16
AF = mybir.ActivationFunctionType
ALU = mybir.AluOpType
AX = mybir.AxisListType

D = 1024
T = 2048
B = 16
NCORE = 8
NDT = 8
NCH = 4
CH = 512
NTT = 16
BF = ml_dtypes.bfloat16

ENGS = ("sync", "tensor", "vector", "scalar", "gpsimd")
QUEUES = {"sync": "qX", "gpsimd": "qW", "scalar": "qO", "vector": "qV"}


def _host_tables():
    t = {}
    idx = np.arange(0, 64, 2).astype(np.float64)
    inv_freq = 1.0 / (10000.0 ** (idx / 64.0))
    pos = np.arange(T, dtype=np.float64)
    ang = inv_freq[:, None] * pos[None, :]
    cos32, sin32 = np.cos(ang), np.sin(ang)
    rep = lambda a: np.tile(a, (4, 1)).astype(np.float32)
    t["cosT"] = rep(cos32)
    t["sinT"] = rep(sin32)
    base = (idx + 0.4 * 64.0) / (1.4 * 64.0)
    power = (pos - T // 2) / 256.0
    scale32 = base[:, None] ** power[None, :]
    t["csT"] = rep(cos32 * scale32)
    t["ssT"] = rep(sin32 * scale32)
    t["ckT"] = rep(cos32 / scale32)
    t["skT"] = rep(sin32 / scale32)
    J = np.zeros((128, 128), dtype=np.float32)
    for g in range(2):
        for i in range(32):
            J[64 * g + i, 64 * g + 32 + i] = -1.0
            J[64 * g + 32 + i, 64 * g + i] = 1.0
    t["JT"] = np.ascontiguousarray(J.T).astype(BF)
    t["ident"] = np.eye(128, dtype=BF)
    oc = np.zeros((1, 128), dtype=BF); oc[0, :] = 1.0
    t["ones_col"] = oc
    t["ones_col_f32"] = oc.astype(np.float32)
    orow = np.ones((1, 512), dtype=BF)
    t["ones_row"] = orow
    s0 = np.zeros((128, 2), dtype=BF); s0[:, 0] = 1.0
    s1 = np.zeros((128, 2), dtype=BF); s1[:, 1] = 1.0
    t["sel_sum0"], t["sel_sum1"] = s0, s1
    hs = np.zeros((128, 2), dtype=BF); hs[0:64, 0] = 1.0; hs[64:128, 1] = 1.0
    t["sel_head"] = hs
    hb2 = np.zeros((2, 128), dtype=BF)
    hb2[0, 0:64] = 1.0
    hb2[1, 64:128] = 1.0
    t["sel_headb"] = hb2
    s8 = np.zeros((8, 8 * 128), dtype=BF)
    for e in range(8):
        s8[e, e * 128:(e + 1) * 128] = 1.0
    t["sel8"] = s8
    for j in range(4):
        m = (np.arange(128)[:, None] <= (np.arange(512)[None, :] - 128 * j))
        t[f"mask_le{j}"] = m.astype(BF)
        m2 = (np.arange(512)[None, :] <= (np.arange(128)[:, None] + 128 * j))
        t[f"mask_ge{j}"] = m2.astype(BF)
    return t


_TABLES = _host_tables()


class Op:
    __slots__ = ("eng", "fn", "seq", "reads", "writes", "is_dma", "queue",
                 "waits", "inc_id", "deps", "dma_idx")

    def __init__(self, eng, fn, reads, writes, is_dma=False, queue=None):
        self.eng = eng
        self.fn = fn
        self.reads = reads
        self.writes = writes
        self.is_dma = is_dma
        self.queue = queue
        self.waits = None
        self.inc_id = None


def _bufs(*aps):
    out = []
    for a in aps:
        if a is None or isinstance(a, (int, float)):
            continue
        out.append(id(a.tensor))
    return out


class K:

    def __init__(self):
        nc = bass.Bass("TRN2", target_bir_lowering=False, debug=False,
                       dynamic_dma_scratch_size=16384)
        self.nc = nc
        self.ops = {e: [] for e in ENGS}
        self.dma_pos = {q: 0 for q in QUEUES.values()}
        self.lastw = {}
        self.readers = {}
        self.dram = {}
        self.input_names = set()
        self._consts = {}
        self._nameid = 0

    def sb(self, name, shape, dtype=F32):
        return self.nc.alloc_sbuf_tensor(name, shape, dtype)

    def psum(self, name, shape=(128, 512), dtype=F32):
        return self.nc.alloc_psum_tensor(name, shape, dtype)

    def const(self, key):
        if key not in self._consts:
            self._consts[key] = self.nc.inline_tensor(
                np.ascontiguousarray(_TABLES[key]), name=f"c_{key}")
        return self._consts[key]

    def din(self, name, shape, dtype=F32):
        if name not in self.dram:
            self.dram[name] = self.nc.dram_tensor(name, list(shape), dtype,
                                                  kind="ExternalInput")
            self.input_names.add(name)
        return self.dram[name]

    def dout(self, name, shape, dtype=F32):
        if name not in self.dram:
            self.dram[name] = self.nc.dram_tensor(name, list(shape), dtype,
                                                  kind="ExternalOutput")
        return self.dram[name]

    def dscratch(self, name, shape, dtype=F32):
        if name not in self.dram:
            self.dram[name] = self.nc.dram_tensor(name, list(shape), dtype,
                                                  kind="Internal")
        return self.dram[name]

    def _record(self, eng, fn, reads, writes, is_dma=False):
        op = Op(eng, fn, reads, writes, is_dma=is_dma,
                queue=QUEUES[eng] if is_dma else None)
        op.seq = len(self.ops[eng])
        if is_dma:
            op.dma_idx = self.dma_pos[op.queue]
            self.dma_pos[op.queue] += 1
        self.ops[eng].append(op)
        deps = {}

        def add(dep):
            if dep is None:
                return
            key = dep.queue if dep.is_dma else dep.eng
            cur = deps.get(key)
            rank = dep.dma_idx if dep.is_dma else dep.seq
            if cur is None or rank > (cur.dma_idx if cur.is_dma else cur.seq):
                deps[key] = dep

        mykey = op.queue if is_dma else eng
        for b in reads:
            for d in self.lastw.get(b, {}).values():
                add(d)
        for b in writes:
            for d in self.lastw.get(b, {}).values():
                add(d)
            for d in self.readers.get(b, {}).values():
                add(d)
        op.deps = [d for d in deps.values()
                   if d.is_dma or (d.eng != eng)]
        if is_dma:
            op.deps = [d for d in op.deps
                       if not (d.is_dma and d.queue == op.queue)]
        for b in writes:
            self.lastw.setdefault(b, {})[mykey] = op
            self.readers[b] = {}
        for b in reads:
            self.readers.setdefault(b, {})[mykey] = op
        return op

    def dma(self, eng, out_ap, in_ap):
        def fn(e, out_ap=out_ap, in_ap=in_ap):
            return e.dma_start(out=out_ap, in_=in_ap)
        return self._record(eng, fn, _bufs(in_ap), _bufs(out_ap), is_dma=True)

    def mmul(self, out, lhsT, rhs, start=True, stop=True):
        def fn(e, out=out, lhsT=lhsT, rhs=rhs, start=start, stop=stop):
            return e.matmul(out, lhsT, rhs, start=start, stop=stop)
        return self._record("tensor", fn, _bufs(lhsT, rhs), _bufs(out))

    def trans(self, out, in_, ident):
        def fn(e, out=out, in_=in_, ident=ident):
            return e.transpose(out, in_, ident)
        return self._record("tensor", fn, _bufs(in_, ident), _bufs(out))

    def copy(self, eng, out, in_):
        if eng == "scalar":
            def fn(e, out=out, in_=in_):
                return e.copy(out=out, in_=in_)
        else:
            def fn(e, out=out, in_=in_):
                return e.tensor_copy(out=out, in_=in_)
        return self._record(eng, fn, _bufs(in_), _bufs(out))

    def memset(self, eng, ap, v):
        def fn(e, ap=ap, v=v):
            return e.memset(ap, v)
        return self._record(eng, fn, [], _bufs(ap))

    def tt(self, eng, out, in0, in1, op):
        def fn(e, out=out, in0=in0, in1=in1, op=op):
            return e.tensor_tensor(out=out, in0=in0, in1=in1, op=op)
        return self._record(eng, fn, _bufs(in0, in1), _bufs(out))

    def ts(self, eng, out, in0, s1, op0, s2=None, op1=None):
        def fn(e, out=out, in0=in0, s1=s1, op0=op0, s2=s2, op1=op1):
            if op1 is None:
                return e.tensor_single_scalar(out=out, in_=in0, scalar=s1, op=op0)
            return e.tensor_scalar(out=out, in0=in0, scalar1=s1, scalar2=s2,
                                   op0=op0, op1=op1)
        return self._record(eng, fn, _bufs(in0, s1 if not isinstance(s1, float) else None,
                                           s2 if not isinstance(s2, (float, type(None))) else None),
                            _bufs(out))

    def stt(self, out, in0, scalar, in1, op0, op1, eng="vector"):
        def fn(e, out=out, in0=in0, scalar=scalar, in1=in1, op0=op0, op1=op1):
            return e.scalar_tensor_tensor(out=out, in0=in0, scalar=scalar,
                                          in1=in1, op0=op0, op1=op1)
        return self._record(eng, fn,
                            _bufs(in0, in1, scalar if not isinstance(scalar, float) else None),
                            _bufs(out))

    def act(self, out, in_, func, bias=0.0, scale=1.0, accum=None):
        def fn(e, out=out, in_=in_, func=func, bias=bias, scale=scale, accum=accum):
            return e.activation(out=out, in_=in_, func=func, bias=bias,
                                scale=scale, accum_out=accum)
        return self._record("scalar", fn,
                            _bufs(in_, bias if not isinstance(bias, float) else None,
                                  scale if not isinstance(scale, float) else None),
                            _bufs(out, accum))

    def red(self, out, in_, op, eng="vector"):
        def fn(e, out=out, in_=in_, op=op):
            return e.tensor_reduce(out=out, in_=in_, axis=AX.X, op=op)
        return self._record(eng, fn, _bufs(in_), _bufs(out))

    def recip(self, out, in_):
        def fn(e, out=out, in_=in_):
            return e.reciprocal(out=out, in_=in_)
        return self._record("vector", fn, _bufs(in_), _bufs(out))

    def fence(self, eng, aps):
        def fn(e):
            return e.nop()
        return self._record(eng, fn, _bufs(*aps), [])

    def build(self):
        NSLOT = 16
        sems = {}
        nc = self.nc
        for e in ENGS:
            sems[e] = nc.alloc_semaphore(name=f"se_{e}")
        qsems = {}
        for q in set(QUEUES.values()):
            qsems[q] = [nc.alloc_semaphore(name=f"sq_{q}_{i}")
                        for i in range(NSLOT)]
        for e in ENGS:
            for op in self.ops[e]:
                for d in op.deps:
                    if not d.is_dma and d.inc_id is None:
                        d.inc_id = -1
        for e in ENGS:
            cnt = 0
            for op in self.ops[e]:
                if op.inc_id == -1:
                    cnt += 1
                    op.inc_id = cnt
        for e in ENGS:
            for op in self.ops[e]:
                w = {}
                for d in op.deps:
                    if d.is_dma:
                        key = (d.queue, d.dma_idx % NSLOT)
                        val = 16 * (d.dma_idx // NSLOT + 1)
                    else:
                        key = d.eng
                        assert d.inc_id and d.inc_id > 0
                        val = d.inc_id
                    if w.get(key, -1) < val:
                        w[key] = val
                op.waits = [((sems[k] if isinstance(k, str) else
                              qsems[k[0]][k[1]]), v) for k, v in w.items()]

                if op.is_dma and op.dma_idx >= 2 * NSLOT:
                    pidx = op.dma_idx - 2 * NSLOT
                    op.waits.append((qsems[op.queue][pidx % NSLOT],
                                     16 * (pidx // NSLOT + 1)))
        allsems = list(sems.values()) + [h for q in qsems.values() for h in q]
        for h in allsems:
            nc.gpsimd.sem_clear(h)
        nc.all_engine_barrier()
        for h in allsems:
            nc.gpsimd.sem_clear(h)
        nc.all_engine_barrier()
        nc.all_engine_barrier()
        with nc.Block() as block:
            for eng_name in ENGS:
                ops = self.ops[eng_name]

                def make(ops=ops, eng_name=eng_name):
                    def run(eng):
                        for op in ops:
                            for sh, v in op.waits:
                                eng.wait_ge(sh, v)
                            ins = op.fn(eng)
                            if op.is_dma:
                                ins = ins.then_inc(
                                    qsems[op.queue][op.dma_idx % 16], 16)
                            elif op.inc_id:
                                eng.drain().then_inc(sems[eng_name], 1)
                    return run

                getattr(block, eng_name)(make())
        return nc


class MK:
    def __init__(self, k: K):
        self.k = k
        self.x8 = [k.sb(f"x{i}", [128, T], F32) for i in range(NDT)]
        self.hb8 = [k.sb(f"hb{i}", [128, T], BF16) for i in range(NDT)]
        self.ob8 = [k.sb(f"ob{i}", [128, T], BF16) for i in range(NDT)]
        self.vplus = k.sb("vplus", [128, 8, 16, 65], BF16)
        self.qr = k.sb("qr", [128, T], BF16)
        self.kr = k.sb("kr", [128, T], BF16)
        self.rope_s = [k.sb(f"rope_s{i}", [128, CH], BF16) for i in range(1)]
        self.rope_c = [k.sb(f"rope_c{i}", [128, CH], BF16) for i in range(1)]
        self.P_sb = [k.sb(f"P{i}", [128, CH], BF16) for i in range(2)]
        self.rb_sb = [k.sb(f"rb{i}", [128, CH], BF16) for i in range(1)]
        self.f32s = [k.sb(f"f32s{i}", [128, CH], F32) for i in range(1)]
        self.xb_st = [k.sb(f"xbst{i}", [128, CH], BF16) for i in range(1)]
        self.x2_st = [k.sb(f"x2st{i}", [128, CH], BF16) for i in range(1)]
        self.rowm = k.sb("rowm", [1, CH], F32)[:]
        self.rowv = k.sb("rowv", [1, CH], F32)[:]
        self.rowt = k.sb("rowt", [1, CH], F32)[:]
        self.rowr = k.sb("rowr", [1, CH], F32)[:]
        self.rowr_bf = k.sb("rowr_bf", [1, CH], BF16)
        self.tabA = k.sb("tabA", [128, T], F32)
        self.tabB = k.sb("tabB", [128, T], F32)
        self.masks = [k.sb(f"mk{i}", [128, CH], BF16) for i in range(8)]
        self.JT = k.sb("JTs", [128, 128], BF16)
        self.sel_sum0 = k.sb("sel0", [128, 2], BF16)
        self.sel_sum1 = k.sb("sel1", [128, 2], BF16)
        self.sel_head = k.sb("selh", [128, 2], BF16)
        self.sel_headb = k.sb("selhb", [2, 128], BF16)
        self.ones_col = k.sb("onesc", [1, 128], BF16)
        self.ones_col_f32 = k.sb("onescf", [1, 128], F32)
        self.ones_row = k.sb("onesr", [1, 512], BF16)
        self.pcols = k.sb("pcols", [128, 48], F32)
        self.brow_bf = k.sb("brow_bf", [1, CH], BF16)
        self.pb = [k.psum(f"pb{i}") for i in range(7)]
        self.pbT = k.psum("pbT", (128, 512), BF16)[:, 0:128]
        self.selb = k.sb("selb", [8, 128], BF16)
        self.wA = k.sb("wA", [128, 8, 512], BF16)


    def bias_tile(self, val, parts=1):
        key = (float(val), parts)
        if not hasattr(self, "_bias_tiles"):
            self._bias_tiles = {}
        if key not in self._bias_tiles:
            t = self.k.sb(f"bias_{len(self._bias_tiles)}", [parts, 1], F32)
            self.k.memset("vector", t[:], float(val))
            self._bias_tiles[key] = t
        return self._bias_tiles[key]

    def load_consts(self):
        k = self.k
        for key, dst in (("JT", self.JT), ("sel_sum0", self.sel_sum0),
                         ("sel_sum1", self.sel_sum1), ("sel_head", self.sel_head),
                         ("sel_headb", self.sel_headb),
                         ("ones_col", self.ones_col), ("ones_row", self.ones_row),
                         ("ones_col_f32", self.ones_col_f32)):
            k.dma("gpsimd", dst[:], k.const(key).ap())

    def load_masks(self, keys):
        k = self.k
        for i, key in enumerate(keys):
            k.dma("gpsimd", self.masks[i][:], k.const(key).ap())

    def load_x(self, b, name="x_fm"):
        k = self.k
        xin = k.din(name, [2, D, T], F32)
        for dt in range(NDT):
            k.dma("sync", self.x8[dt][:], xin.ap()[b, dt * 128:(dt + 1) * 128, :])

    def store_x(self, b, name="y_fm"):
        k = self.k
        yout = k.dout(name, [2, D, T], F32)
        for dt in range(NDT):
            k.dma("scalar", yout.ap()[b, dt * 128:(dt + 1) * 128, :], self.x8[dt][:])

    def finish(self):
        k = self.k
        if "y_fm" in k.dram:
            k.fence("sync", [k.dram["y_fm"].ap()])

    def dump8(self, b, tiles, dtype=F32):
        k = self.k
        tap = k.dout("tap", [2, D, T], dtype)
        for dt in range(NDT):
            k.dma("gpsimd", tap.ap()[b, dt * 128:(dt + 1) * 128, :],
                  tiles[dt][:] if not isinstance(tiles[dt], bass.AP) else tiles[dt])

    def park_x(self, b):
        k = self.k
        xp = k.dscratch("x_park", [2, D, T], F32)
        for dt in range(NDT):
            k.dma("scalar", xp.ap()[b, dt * 128:(dt + 1) * 128, :],
                  self.x8[dt][:])

    def unpark_add(self, b):
        k = self.k
        xp = k.dscratch("x_park", [2, D, T], F32)
        for dt in range(NDT):
            for c in range(NCH):
                cs = slice(c * CH, (c + 1) * CH)
                f = self.f32s[0]
                k.dma("sync", f[:], xp.ap()[b, dt * 128:(dt + 1) * 128, cs])
                k.tt("vector", self.x8[dt][:, cs], self.x8[dt][:, cs], f[:],
                     ALU.add)

    def ln_inplace(self, eps=1e-5):
        k = self.k
        psx = self.pb[4]
        psq = self.pb[5]
        pbc = self.pb[6]
        sel0 = self.sel_sum0[:, 0:1]
        for c in range(NCH):
            cs = slice(c * CH, (c + 1) * CH)
            for dt in range(NDT):
                xb = self.xb_st[0]
                x2 = self.x2_st[0]
                k.copy("vector", xb[:], self.x8[dt][:, cs])
                k.act(x2[:], xb[:], AF.Square)
                k.mmul(psx[0:1, :], sel0, xb[:],
                       start=(dt == 0), stop=(dt == NDT - 1))
                k.mmul(psq[0:1, :], sel0, x2[:],
                       start=(dt == 0), stop=(dt == NDT - 1))
            k.ts("vector", self.rowm, psx[0:1, :], 1.0 / D, ALU.mult)
            k.ts("vector", self.rowv, psq[0:1, :], 1.0 / D, ALU.mult)
            k.tt("vector", self.rowt, self.rowm, self.rowm, ALU.mult)
            k.tt("vector", self.rowv, self.rowv, self.rowt, ALU.subtract)
            k.act(self.rowt, self.rowv, AF.Ln, bias=self.bias_tile(eps)[:])
            k.act(self.rowr, self.rowt, AF.Exp, scale=-0.5)
            k.mmul(pbc[:, :], self.ones_col_f32[:], self.rowm)
            k.mmul(psq[:, :], self.ones_col_f32[:], self.rowr)
            for dt in range(NDT):
                f = self.f32s[0]
                k.tt("vector", f[:], self.x8[dt][:, cs], pbc[:, :],
                     ALU.subtract)
                k.tt("vector", self.x8[dt][:, cs], f[:], psq[:, :], ALU.mult)

    def ln_plain(self, src8, dst8, eps=1e-5):
        k = self.k
        psx = self.pb[4]
        psq = self.pb[5]
        pbc = self.pb[6]
        sel0 = self.sel_sum0[:, 0:1]
        for c in range(NCH):
            cs = slice(c * CH, (c + 1) * CH)
            for dt in range(NDT):
                xb = self.xb_st[0]
                x2 = self.x2_st[0]
                k.copy("vector", xb[:], src8[dt][:, cs])
                k.act(x2[:], xb[:], AF.Square)
                k.mmul(psx[0:1, :], sel0, xb[:],
                       start=(dt == 0), stop=(dt == NDT - 1))
                k.mmul(psq[0:1, :], sel0, x2[:],
                       start=(dt == 0), stop=(dt == NDT - 1))
            k.ts("vector", self.rowm, psx[0:1, :], 1.0 / D, ALU.mult)
            k.ts("vector", self.rowv, psq[0:1, :], 1.0 / D, ALU.mult)
            k.tt("vector", self.rowt, self.rowm, self.rowm, ALU.mult)
            k.tt("vector", self.rowv, self.rowv, self.rowt,
                 ALU.subtract)
            k.act(self.rowt, self.rowv, AF.Ln,
                  bias=self.bias_tile(eps)[:])
            k.act(self.rowr, self.rowt, AF.Exp, scale=-0.5)
            k.mmul(pbc[:, :], self.ones_col_f32[:], self.rowm)
            k.mmul(psq[:, :], self.ones_col_f32[:], self.rowr)
            for dt in range(NDT):
                f = self.f32s[0]
                k.tt("vector", f[:], src8[dt][:, cs], pbc[:, :], ALU.subtract)
                k.tt("vector", dst8[dt][:, cs], f[:], psq[:, :], ALU.mult)

    def s0(self, b):
        k = self.k
        WqT = k.din("L0_attn_WqT", [D, D])
        WkT = k.din("L0_attn_WkT", [D, D])
        WvT = k.din("L0_attn_WvT", [D, D])
        WoT = k.din("L0_attn_WoT", [D, D])
        k.dma("gpsimd", self.tabA[:], k.const("cosT").ap())
        k.dma("gpsimd", self.tabB[:], k.const("sinT").ap())
        k.dma("gpsimd", self.JT[:], k.const("JT").ap())
        self.load_masks([f"mask_le{j}" for j in range(4)])
        self.ln_plain(self.x8, self.hb8)
        k.memset("vector", self.vplus[:, :, :, 64:65], 1.0)
        for half in range(2):
            wv = self.wA[:]
            hcs = slice(half * 512, half * 512 + 512)
            k.dma("gpsimd", wv,
                  WvT.ap()[:, hcs].rearrange("(a p) n -> p a n", p=128))
            if TAP == "wv0" and half == 0:
                tap = k.dout("tap", [2, 128, 8 * 512], BF16)
                k.dma("sync", tap.ap()[b, :, :],
                      wv.rearrange("p a b -> p (a b)"))
            if TAP == "hb0" and half == 0:
                tap = k.dout("tap", [2, D, T], BF16)
                for dt in range(NDT):
                    k.dma("sync", tap.ap()[b, dt * 128:(dt + 1) * 128, :],
                          self.hb8[dt][:])
            for tt_ in range(NTT):
                ps = self.pb[tt_ % 2]
                tsl = slice(tt_ * 128, (tt_ + 1) * 128)
                for dt in range(NDT):
                    k.mmul(ps[:, :], self.hb8[dt][:, tsl], wv[:, dt, :],
                           start=(dt == 0), stop=(dt == NDT - 1))
                k.copy("vector", self.vplus[:, :, tt_, 0:64],
                       ps[:].rearrange("p (a n) -> p a n", a=8))
            wAf = self.wA[:].rearrange("p a n -> p (a n)")
            for pr in range(4):
                pair = half * 4 + pr
                wq = wAf[:, 0:1024].rearrange("p (a n) -> p a n", a=8)
                wk = wAf[:, 1024:2048].rearrange("p (a n) -> p a n", a=8)
                psl = slice(pair * 128, (pair + 1) * 128)
                k.dma("gpsimd", wq,
                      WqT.ap()[:, psl].rearrange("(a p) n -> p a n", p=128))
                k.dma("gpsimd", wk,
                      WkT.ap()[:, psl].rearrange("(a p) n -> p a n", p=128))
                for c in range(NCH):
                    cs = slice(c * CH, (c + 1) * CH)
                    for (w8, dst) in ((wq, self.qr), (wk, self.kr)):
                        pq = self.pb[2]
                        pj = self.pb[3]
                        for dt in range(NDT):
                            k.mmul(pq[:, :], w8[:, dt, :], self.hb8[dt][:, cs],
                                   start=(dt == 0), stop=(dt == NDT - 1))
                        tsn = self.rope_s[0]
                        tcs = self.rope_c[0]
                        k.tt("vector", tsn[:], pq[:, :], self.tabB[:, cs], ALU.mult)
                        k.tt("vector", tcs[:], pq[:, :], self.tabA[:, cs], ALU.mult)
                        k.mmul(pj[:, :], self.JT[:], tsn[:])
                        k.stt(dst[:, cs], pj[:, :], 1.0, tcs[:],
                              ALU.mult, ALU.add)
                for hh in range(2):
                    hrows = slice(hh * 64, hh * 64 + 64)
                    vh = pr * 2 + hh
                    for qc in range(NCH):
                        qcs = slice(qc * CH, (qc + 1) * CH)
                        pav = self.pb[4]
                        nkt = 4 * qc + 4
                        for kt in range(nkt):
                            psc = self.pb[5 + kt % 2]
                            P = self.P_sb[kt % 2]
                            k.mmul(psc[:, :],
                                   self.kr[hrows, kt * 128:(kt + 1) * 128],
                                   self.qr[hrows, qcs])
                            k.act(P[:], psc[:, :], AF.Exp, scale=0.125)
                            j = kt - 4 * qc
                            if j >= 0:
                                k.tt("vector", P[:], P[:], self.masks[j][:],
                                     ALU.mult)
                            k.mmul(pav[0:65, :], self.vplus[:, vh, kt, :], P[:],
                                   start=(kt == 0), stop=(kt == nkt - 1))
                        k.act(self.rowt, pav[64:65, :], AF.Ln)
                        k.act(self.rowr_bf[0:1, :], self.rowt, AF.Exp,
                              scale=-1.0)
                        prb = self.pb[3]
                        k.mmul(prb[0:64, :], self.ones_col[:, 0:64],
                               self.rowr_bf[0:1, :])
                        rb = self.rb_sb[0]
                        k.copy("vector", rb[0:64, :], prb[0:64, :])
                        k.tt("vector", self.ob8[pair][hrows, qcs],
                             pav[0:64, :], rb[0:64, :], ALU.mult)
        wo = self.wA[:]
        for wh in range(2):
            wcs = slice(wh * 512, wh * 512 + 512)
            k.dma("gpsimd", wo,
                  WoT.ap()[:, wcs].rearrange("(a p) n -> p a n", p=128))
            for jt2 in range(4):
                jt = wh * 4 + jt2
                for c in range(NCH):
                    cs = slice(c * CH, (c + 1) * CH)
                    ps = self.pb[c % 2]
                    for dt in range(NDT):
                        k.mmul(ps[:, :], wo[:, dt, jt2 * 128:(jt2 + 1) * 128],
                               self.ob8[dt][:, cs],
                               start=(dt == 0), stop=(dt == NDT - 1))
                    k.tt("vector", self.x8[jt][:, cs], self.x8[jt][:, cs],
                         ps[:, :], ALU.add)

    def s1(self, b):
        k = self.k
        wqkv = k.din("L0_ffn_wqkv", [2, D, 1536])
        wo_d = k.din("L0_ffn_wo", [2, 512, D])
        w1_d = k.din("L0_ffn_w1", [2, D, 5460])
        w2_d = k.din("L0_ffn_w2", [2, 2730, D])
        b1_r = k.din("L0_ffn_b1_r", [2, 5460])
        qs_p = k.din("L0_ffn_qs_p", [2, 128, 1])
        ks_p = k.din("L0_ffn_ks_p", [2, 128, 1])
        lg_p = k.din("L0_ffn_lg_p", [2, 128, 8])
        lb_p = k.din("L0_ffn_lb_p", [2, 128, 8])
        b2_p = k.din("L0_ffn_b2_p", [2, 128, 8])
        self.load_masks([f"mask_le{j}" for j in range(4)]
                        + [f"mask_ge{j}" for j in range(4)])
        k.dma("gpsimd", self.JT[:], k.const("JT").ap())
        self.park_x(b)
        self.ln_inplace()
        wAf = self.wA[:].rearrange("p a n -> p (a n)")
        HID = 2730
        import kernel as _K
        for l in range(2):
            k.dma("sync", self.pcols[:, 0:1], qs_p.ap()[l])
            k.dma("sync", self.pcols[:, 1:2], ks_p.ap()[l])
            k.dma("sync", self.pcols[:, 8:16], lg_p.ap()[l])
            k.dma("sync", self.pcols[:, 16:24], lb_p.ap()[l])
            k.dma("sync", self.pcols[:, 24:32], b2_p.ap()[l])
            for dt in range(NDT):
                if dt < 4:
                    k.copy("vector", self.hb8[dt][:], self.x8[dt][:])
                else:
                    k.memset("vector", self.hb8[dt][:, 0:8], 0.0)
                    k.copy("vector", self.hb8[dt][:, 1:T],
                           self.x8[dt][:, 0:T - 1])
            wv = self.wA[:]
            k.dma("gpsimd", wv, wqkv.ap()[l, :, 1024:1536]
                  .rearrange("(a p) n -> p a n", p=128))
            k.memset("vector", self.vplus[:, :, :, 64:65], 1.0)
            for tt_ in range(NTT):
                ps = self.pb[tt_ % 2]
                tsl = slice(tt_ * 128, (tt_ + 1) * 128)
                for dt in range(NDT):
                    k.mmul(ps[:, :], self.hb8[dt][:, tsl], wv[:, dt, :],
                           start=(dt == 0), stop=(dt == NDT - 1))
                k.copy("vector", self.vplus[:, :, tt_, 0:64],
                       ps[:].rearrange("p (a n) -> p a n", a=8))
            for pr in range(4):
                wq = wAf[:, 0:1024].rearrange("p (a n) -> p a n", a=8)
                wk = wAf[:, 1024:2048].rearrange("p (a n) -> p a n", a=8)
                k.dma("gpsimd", wq, wqkv.ap()[l, :, pr * 128:(pr + 1) * 128]
                      .rearrange("(a p) n -> p a n", p=128))
                k.dma("gpsimd", wk,
                      wqkv.ap()[l, :, 512 + pr * 128:512 + (pr + 1) * 128]
                      .rearrange("(a p) n -> p a n", p=128))
                for c in range(NCH):
                    cs = slice(c * CH, (c + 1) * CH)
                    k.dma("gpsimd", self.tabA[:, 0:512],
                          k.const("csT").ap()[:, cs])
                    k.dma("gpsimd", self.tabA[:, 512:1024],
                          k.const("ssT").ap()[:, cs])
                    k.dma("gpsimd", self.tabA[:, 1024:1536],
                          k.const("ckT").ap()[:, cs])
                    k.dma("gpsimd", self.tabA[:, 1536:2048],
                          k.const("skT").ap()[:, cs])
                    for (w8, dst, scol, ci, si) in (
                            (wq, self.qr, self.pcols[:, 0:1], 0, 512),
                            (wk, self.kr, self.pcols[:, 1:2], 1024, 1536)):
                        pq = self.pb[2]
                        for dt in range(NDT):
                            k.mmul(pq[:, :], w8[:, dt, :], self.hb8[dt][:, cs],
                                   start=(dt == 0), stop=(dt == NDT - 1))
                        k.act(self.x2_st[0][:], pq[:, :], AF.Square)
                        pss = self.pb[4]
                        k.mmul(pss[0:2, :], self.sel_head[:], self.x2_st[0][:])
                        rr = self.f32s[0][0:2, :]
                        k.act(rr, pss[0:2, :], AF.Ln)
                        k.act(rr, rr, AF.Exp, scale=0.5)
                        k.ts("vector", rr, rr, 1e-12, ALU.max)
                        k.act(rr, rr, AF.Ln)
                        k.act(rr, rr, AF.Exp, scale=-1.0)
                        rb2 = self.x2_st[0][0:2, :]
                        k.copy("vector", rb2, rr)
                        pbc2 = self.pb[5]
                        k.mmul(pbc2[:, :], self.sel_headb[:], rb2)
                        rbs = self.xb_st[0]
                        k.copy("vector", rbs[:], pbc2[:, :])
                        qn = self.P_sb[0]
                        k.stt(qn[:], pq[:, :], scol, rbs[:], ALU.mult, ALU.mult)
                        tsn = self.rope_s[0]
                        tcs = self.rope_c[0]
                        k.tt("vector", tsn[:], qn[:],
                             self.tabA[:, si:si + 512], ALU.mult)
                        k.tt("vector", tcs[:], qn[:],
                             self.tabA[:, ci:ci + 512], ALU.mult)
                        pj = self.pb[3]
                        k.mmul(pj[:, :], self.JT[:], tsn[:])
                        k.stt(dst[:, cs], pj[:, :], 1.0, tcs[:],
                              ALU.mult, ALU.add)
                for hh in range(2):
                    hrows = slice(hh * 64, hh * 64 + 64)
                    vhead = pr * 2 + hh
                    for w in range(4):
                        qcs = slice(w * CH, (w + 1) * CH)
                        pav = self.pb[6]
                        kts = ([] if w == 0 else
                               list(range(4 * (w - 1), 4 * w))) + \
                              list(range(4 * w, 4 * w + 4))
                        for idx, kt in enumerate(kts):
                            psc = self.pb[2 + idx % 2]
                            P = self.P_sb[idx % 2]
                            k.mmul(psc[:, :],
                                   self.kr[hrows, kt * 128:(kt + 1) * 128],
                                   self.qr[hrows, qcs])
                            k.act(P[:], psc[:, :], AF.Exp, scale=8.0,
                                  bias=self.bias_tile(-10.0, 128)[:])
                            loc = kt - 4 * w
                            if loc >= 0:
                                k.tt("vector", P[:], P[:],
                                     self.masks[loc][:], ALU.mult)
                            else:
                                k.tt("vector", P[:], P[:],
                                     self.masks[4 + kt - 4 * (w - 1)][:],
                                     ALU.mult)
                            k.mmul(pav[0:65, :], self.vplus[:, vhead, kt, :],
                                   P[:], start=(idx == 0),
                                   stop=(idx == len(kts) - 1))
                        k.act(self.rowt, pav[64:65, :], AF.Ln)
                        k.act(self.rowr_bf[0:1, :], self.rowt, AF.Exp,
                              scale=-1.0)
                        prb = self.pb[3]
                        k.mmul(prb[0:64, :], self.ones_col[:, 0:64],
                               self.rowr_bf[0:1, :])
                        rb = self.rb_sb[0]
                        k.copy("vector", rb[0:64, :], prb[0:64, :])
                        k.tt("vector", self.ob8[pr][hrows, qcs],
                             pav[0:64, :], rb[0:64, :], ALU.mult)
            wov = wAf[:, 0:4096].rearrange("p (a n) -> p a n", a=4)
            k.dma("gpsimd", wov,
                  wo_d.ap()[l].rearrange("(a p) n -> p a n", p=128))
            for jt in range(NDT):
                for c in range(NCH):
                    cs = slice(c * CH, (c + 1) * CH)
                    ps = self.pb[c % 2]
                    for dt in range(4):
                        k.mmul(ps[:, :], wov[:, dt, jt * 128:(jt + 1) * 128],
                               self.ob8[dt][:, cs],
                               start=(dt == 0), stop=(dt == 3))
                    k.tt("vector", self.x8[jt][:, cs], self.x8[jt][:, cs],
                         ps[:, :], ALU.add)
            if _K.TAP == "s1x1" and l == 0:
                self.dump8(b, self.x8)
                return
            psx = self.pb[4]
            psq = self.pb[5]
            pbc = self.pb[6]
            sel0 = self.sel_sum0[:, 0:1]
            for c in range(NCH):
                cs = slice(c * CH, (c + 1) * CH)

                def shifted_into(dst_ap, dt, c=c, cs=cs):
                    if dt < 4:
                        k.copy("vector", dst_ap, self.x8[dt][:, cs])
                    elif c == 0:
                        k.memset("vector", dst_ap[:, 0:8], 0.0)
                        k.copy("vector", dst_ap[:, 1:512],
                               self.x8[dt][:, 0:511])
                    else:
                        k.copy("vector", dst_ap,
                               self.x8[dt][:, c * CH - 1:(c + 1) * CH - 1])

                for dt in range(NDT):
                    xsh = self.f32s[0][:]
                    shifted_into(xsh, dt)
                    xb = self.xb_st[0]
                    x2 = self.x2_st[0]
                    k.copy("vector", xb[:], xsh)
                    k.act(x2[:], xb[:], AF.Square)
                    k.mmul(psx[0:1, :], sel0, xb[:],
                           start=(dt == 0), stop=(dt == NDT - 1))
                    k.mmul(psq[0:1, :], sel0, x2[:],
                           start=(dt == 0), stop=(dt == NDT - 1))
                k.ts("vector", self.rowm, psx[0:1, :], 1.0 / D, ALU.mult)
                k.ts("vector", self.rowv, psq[0:1, :], 1.0 / D, ALU.mult)
                k.tt("vector", self.rowt, self.rowm, self.rowm, ALU.mult)
                k.tt("vector", self.rowv, self.rowv, self.rowt, ALU.subtract)
                k.act(self.rowt, self.rowv, AF.Ln,
                      bias=self.bias_tile(1e-5)[:])
                k.act(self.rowr, self.rowt, AF.Exp, scale=-0.5)
                k.mmul(pbc[:, :], self.ones_col_f32[:], self.rowm)
                k.mmul(psq[:, :], self.ones_col_f32[:], self.rowr)
                for dt in range(NDT):
                    xsh = self.f32s[0][:]
                    shifted_into(xsh, dt)
                    k.tt("vector", xsh, xsh, pbc[:, :], ALU.subtract)
                    k.stt(self.xb_st[0][:], xsh,
                          self.pcols[:, 8 + dt:9 + dt], psq[:, :],
                          ALU.mult, ALU.mult)
                    k.ts("vector", self.hb8[dt][:, cs], self.xb_st[0][:],
                         self.pcols[:, 16 + dt:17 + dt], ALU.add)
            if _K.TAP == "s1h2" and l == 0:
                self.dump8(b, self.hb8, BF16)
                return
            NHT = 22
            wa_blk = wAf[:, 0:1024].rearrange("p (a n) -> p a n", a=8)
            wg_blk = wAf[:, 1024:2048].rearrange("p (a n) -> p a n", a=8)
            w2c = wAf[:, 2048:4096]

            def midsl(i):
                return self.ob8[i // 4][:, (i % 4) * 512:(i % 4 + 1) * 512]

            for c in range(NCH):
                cs = slice(c * CH, (c + 1) * CH)
                for i in range(NHT):
                    wcols = 128 if i < 21 else 42
                    h0 = i * 128
                    k.dma("gpsimd", wa_blk[:, :, 0:wcols],
                          w1_d.ap()[l, :, h0:h0 + wcols]
                          .rearrange("(a p) n -> p a n", p=128))
                    k.dma("gpsimd", wg_blk[:, :, 0:wcols],
                          w1_d.ap()[l, :, HID + h0:HID + h0 + wcols]
                          .rearrange("(a p) n -> p a n", p=128))
                    k.dma("gpsimd", self.brow_bf[0:1, 0:wcols],
                          b1_r.ap()[l:l + 1, h0:h0 + wcols])
                    k.dma("gpsimd", self.brow_bf[0:1, 128:128 + wcols],
                          b1_r.ap()[l:l + 1, HID + h0:HID + h0 + wcols])
                    pa = self.pb[2]
                    pg = self.pb[3]
                    for dt in range(NDT):
                        k.mmul(pa[0:wcols, :], wa_blk[:, dt, 0:wcols],
                               self.hb8[dt][:, cs],
                               start=(dt == 0), stop=False)
                    k.mmul(pa[0:wcols, :], self.brow_bf[0:1, 0:wcols],
                           self.ones_row[:], start=False, stop=True)
                    for dt in range(NDT):
                        k.mmul(pg[0:wcols, :], wg_blk[:, dt, 0:wcols],
                               self.hb8[dt][:, cs],
                               start=(dt == 0), stop=False)
                    k.mmul(pg[0:wcols, :], self.brow_bf[0:1, 128:128 + wcols],
                           self.ones_row[:], start=False, stop=True)
                    gel = self.x2_st[0]
                    k.act(gel[0:wcols, :], pg[0:wcols, :], AF.Gelu)
                    k.tt("vector", midsl(i)[0:wcols, :], pa[0:wcols, :],
                         gel[0:wcols, :], ALU.mult)
                for j in range(NDT):
                    w2v = w2c.rearrange("p (a n) -> p a n", a=16)
                    w2t = wAf[:, 0:640].rearrange("p (a n) -> p a n", a=5)
                    w2u = wAf[:, 640:768]
                    jc = slice(j * 128, (j + 1) * 128)
                    k.dma("gpsimd", w2v,
                          w2_d.ap()[l, 0:2048, jc]
                          .rearrange("(a p) n -> p a n", p=128))
                    k.dma("gpsimd", w2t,
                          w2_d.ap()[l, 2048:2688, jc]
                          .rearrange("(a p) n -> p a n", p=128))
                    k.dma("gpsimd", w2u[0:42, :], w2_d.ap()[l, 2688:2730, jc])
                    ps = self.pb[4 + j % 2]
                    for i in range(NHT):
                        wcols = 128 if i < 21 else 42
                        lt = (w2v[:, i, :] if i < 16 else
                              (w2t[:, i - 16, :] if i < 21 else w2u))
                        k.mmul(ps[:, :], lt[0:wcols, :], midsl(i)[0:wcols, :],
                               start=(i == 0), stop=(i == NHT - 1))
                    k.stt(self.x8[j][:, cs], ps[:, :],
                          self.pcols[:, 24 + j:25 + j],
                          self.x8[j][:, cs], ALU.add, ALU.add)
            if _K.TAP == "s1x2" and l == 0:
                self.dump8(b, self.x8)
                return
        self.unpark_add(b)

    def s2(self, b):
        k = self.k
        WqT = k.din("L1_attn_WqT", [D, D])
        WkT = k.din("L1_attn_WkT", [D, D])
        WvT = k.din("L1_attn_WvT", [D, D])
        WuT = k.din("L1_attn_WuT", [D, D])
        WoT = k.din("L1_attn_WoT", [D, D])
        bq = k.din("L1_attn_bq_p", [128, 8])
        bk = k.din("L1_attn_bk_p", [128, 8])
        bv = k.din("L1_attn_bv_r", [1, D])
        bu = k.din("L1_attn_bu_p", [128, 8])
        bo = k.din("L1_attn_bo_p", [128, 8])
        scale = k.din("L1_attn_scale_p", [128, 8])
        self.load_masks([f"mask_le{j}" for j in range(4)])
        k.dma("sync", self.pcols[:, 0:8], bq.ap())
        k.dma("sync", self.pcols[:, 8:16], bk.ap())
        k.dma("sync", self.pcols[:, 16:24], bu.ap())
        k.dma("sync", self.pcols[:, 24:32], bo.ap())
        k.dma("sync", self.pcols[:, 32:40], scale.ap())
        self.ln_plain(self.x8, self.hb8)
        wAflat = self.wA[:].rearrange("p a n -> p (a n)")
        vh = wAflat[:, 0:2048].rearrange("p (a n) -> p a n", a=16)
        for h in range(NDT):
            hcol = slice(h * 128, (h + 1) * 128)
            wq = wAflat[:, 2048:3072].rearrange("p (a n) -> p a n", a=8)
            k.dma("gpsimd", wq, WqT.ap()[:, hcol].rearrange("(a p) n -> p a n", p=128))

            for c in range(NCH):
                cs = slice(c * CH, (c + 1) * CH)
                ps = self.pb[2]
                for kt in range(NDT):
                    k.mmul(ps[:, :], wq[:, kt, :], self.hb8[kt][:, cs],
                           start=(kt == 0), stop=(kt == NDT - 1))
                self._elu(ps, self.pcols[:, 0 + h:1 + h], self.qr[:, cs])
            k.dma("gpsimd", wq, WkT.ap()[:, hcol].rearrange("(a p) n -> p a n", p=128))

            for c in range(NCH):
                cs = slice(c * CH, (c + 1) * CH)
                ps = self.pb[2]
                for kt in range(NDT):
                    k.mmul(ps[:, :], wq[:, kt, :], self.hb8[kt][:, cs],
                           start=(kt == 0), stop=(kt == NDT - 1))
                self._elu(ps, self.pcols[:, 8 + h:9 + h], self.kr[:, cs])
            k.dma("gpsimd", wq, WvT.ap()[:, hcol].rearrange("(a p) n -> p a n", p=128))
            k.dma("gpsimd", self.brow_bf[0:1, 0:128], bv.ap()[:, hcol])
            for tt_ in range(NTT):
                tsl = slice(tt_ * 128, (tt_ + 1) * 128)
                ps = self.pb[3]
                for kt in range(NDT):
                    k.mmul(ps[:, 0:128], self.hb8[kt][:, tsl], wq[:, kt, :],
                           start=(kt == 0), stop=False)
                k.mmul(ps[:, 0:128], self.ones_col[:],
                       self.brow_bf[0:1, 0:128], start=False, stop=True)
                k.act(vh[:, tt_, :], ps[:, 0:128], AF.Silu)
            for qc in range(NCH):
                qcs = slice(qc * CH, (qc + 1) * CH)
                pav = self.pb[4]
                nkt = 4 * qc + 4
                for kt in range(nkt):
                    psc = self.pb[5 + kt % 2]
                    E = self.P_sb[kt % 2]
                    k.mmul(psc[:, :], self.kr[:, kt * 128:(kt + 1) * 128],
                           self.qr[:, qcs])
                    j = kt - 4 * qc
                    if j >= 0:
                        k.tt("vector", E[:], psc[:, :], self.masks[j][:],
                             ALU.mult)
                    else:
                        k.copy("scalar", E[:], psc[:, :])
                    k.mmul(pav[:, :], vh[:, kt, :], E[:],
                           start=(kt == 0), stop=(kt == nkt - 1))
                k.copy("vector", self.ob8[h][:, qcs], pav[:, :])
        psx = self.pb[5]
        sel0 = self.sel_sum0[:, 0:1]
        wo = self.wA[:]
        for c in range(NCH):
            cs = slice(c * CH, (c + 1) * CH)
            for dt in range(NDT):
                k.act(self.x2_st[0][:], self.ob8[dt][:, cs], AF.Square)
                k.mmul(psx[0:1, :], sel0, self.x2_st[0][:],
                       start=(dt == 0), stop=(dt == NDT - 1))
            k.act(self.rowt, psx[0:1, :], AF.Ln, scale=1.0 / D)
            k.act(self.rowm, self.rowt, AF.Exp, scale=0.5)
            k.ts("vector", self.rowm, self.rowm, 1e-8, ALU.add)
            k.act(self.rowt, self.rowm, AF.Ln)
            k.act(self.rowr, self.rowt, AF.Exp, scale=-1.0)
            k.mmul(psx[:, :], self.ones_col_f32[:], self.rowr)
            uslots = [self.rope_s[0][:], self.rope_c[0][:],
                      self.P_sb[0][:], self.P_sb[1][:]]
            for half in range(2):
                k.dma("gpsimd", wo, WuT.ap()[:, half * 512:(half + 1) * 512]
                      .rearrange("(a p) n -> p a n", p=128))
                for dt2 in range(4):
                    dt = half * 4 + dt2
                    pu = self.pb[6]
                    for kt in range(NDT):
                        k.mmul(pu[:, :], wo[:, kt, dt2 * 128:(dt2 + 1) * 128],
                               self.hb8[kt][:, cs],
                               start=(kt == 0), stop=(kt == NDT - 1))
                    k.ts("vector", self.f32s[0][:], pu[:, :],
                         self.pcols[:, 16 + dt:17 + dt], ALU.add)
                    k.act(uslots[dt2], self.f32s[0][:], AF.Silu)
                    zb = (self.qr if dt < 4 else self.kr)[:, (dt % 4) * 512:
                                                          (dt % 4) * 512 + 512]
                    k.tt("vector", self.f32s[0][:], self.ob8[dt][:, cs],
                         psx[:, :], ALU.mult)
                    k.stt(zb, self.f32s[0][:], self.pcols[:, 32 + dt:33 + dt],
                          uslots[dt2], ALU.mult, ALU.mult)
            for wh in range(2):
                k.dma("gpsimd", wo, WoT.ap()[:, wh * 512:(wh + 1) * 512]
                      .rearrange("(a p) n -> p a n", p=128))
                for jt2 in range(4):
                    jt = wh * 4 + jt2
                    ps = self.pb[jt2 % 2]
                    for dt in range(NDT):
                        zb = (self.qr if dt < 4 else self.kr)[:, (dt % 4) * 512:
                                                              (dt % 4) * 512 + 512]
                        k.mmul(ps[:, :], wo[:, dt, jt2 * 128:(jt2 + 1) * 128],
                               zb, start=(dt == 0), stop=(dt == NDT - 1))
                    k.stt(self.x8[jt][:, cs], ps[:, :],
                          self.pcols[:, 24 + jt:25 + jt],
                          self.x8[jt][:, cs], ALU.add, ALU.add)

    def _elu(self, ps, bcol, dst):
        k = self.k
        zmin = self.rope_s[0]
        zmax = self.rope_c[0]
        e = self.P_sb[0]
        k.ts("vector", zmin[:], ps[:, :], bcol, ALU.add, 0.0, ALU.min)
        k.ts("vector", zmax[:], ps[:, :], bcol, ALU.add, 0.0, ALU.max)
        k.act(e[:], zmin[:], AF.Exp)
        k.stt(dst, e[:], -1.0, zmax[:], ALU.add, ALU.add)

    def s3(self, b):
        k = self.k
        Wq = k.din("L1_ffn_Wq", [D, D])
        Wdown = k.din("L1_ffn_Wdown", [D, 64])
        Wup = k.din("L1_ffn_Wup", [64, 2048])
        Wo = k.din("L1_ffn_Wo", [D, D])
        Wg = k.din("L1_ffn_Wg", [D, 8])
        We = k.din("L1_ffn_We", [8, D, D])
        bq_r = k.din("L1_ffn_bq_r", [1, D])
        bdown_r = k.din("L1_ffn_bdown_r", [1, 64])
        bup_r = k.din("L1_ffn_bup_r", [1, 2048])
        bg_r = k.din("L1_ffn_bg_r", [1, 8])
        g1p = k.din("L1_ffn_gamma1_p", [128, 8])
        g2p = k.din("L1_ffn_gamma2_p", [128, 8])
        bop = k.din("L1_ffn_bo_p", [128, 8])
        k.dma("sync", self.pcols[:, 0:8], g1p.ap())
        k.dma("sync", self.pcols[:, 8:16], g2p.ap())
        k.dma("sync", self.pcols[:, 16:24], bop.ap())
        k.dma("gpsimd", self.JT[:], k.const("ident").ap())
        self.park_x(b)
        self.ln_inplace()
        psx = self.pb[6]
        sel0 = self.sel_sum0[:, 0:1]
        f32 = self.f32s[0][:]

        def rms_to_hb(gcol_base, eps):
            for c in range(NCH):
                cs = slice(c * CH, (c + 1) * CH)
                for dt in range(NDT):
                    k.act(self.x2_st[0][:], self.x8[dt][:, cs], AF.Square)
                    k.mmul(psx[0:1, :], sel0, self.x2_st[0][:],
                           start=(dt == 0), stop=(dt == NDT - 1))
                k.act(self.rowt, psx[0:1, :], AF.Ln, scale=1.0 / D,
                      bias=self.bias_tile(eps)[:])
                k.act(self.rowr, self.rowt, AF.Exp, scale=-0.5)
                k.mmul(psx[:, :], self.ones_col_f32[:], self.rowr)
                for dt in range(NDT):
                    k.stt(self.hb8[dt][:, cs], self.x8[dt][:, cs],
                          self.pcols[:, gcol_base + dt:gcol_base + dt + 1],
                          psx[:, :], ALU.mult, ALU.mult)

        import kernel as _K
        if _K.TAP == "s3ln":
            self.dump8(b, self.x8)
            return
        rms_to_hb(0, 1e-6)
        if _K.TAP == "s3h1":
            self.dump8(b, self.hb8, BF16)
            return
        tA = self.tabA[:].bitcast(BF16)
        tB = self.tabB[:].bitcast(BF16)
        vp = self.vplus[:].rearrange("p a b c -> p (a b c)")

        def qslot(tt_):
            if tt_ < 4:
                return tA[:, tt_ * 1024:(tt_ + 1) * 1024]
            if tt_ < 8:
                return tB[:, (tt_ - 4) * 1024:(tt_ - 3) * 1024]
            return vp[:, (tt_ - 8) * 1024:(tt_ - 7) * 1024]

        wq = self.wA[:]
        for oc in range(2):
            ocs = slice(oc * 512, (oc + 1) * 512)
            k.dma("gpsimd", wq, Wq.ap()[:, ocs].rearrange("(a p) n -> p a n", p=128))
            k.dma("gpsimd", self.brow_bf[0:1, :], bq_r.ap()[:, ocs])
            for tt_ in range(NTT):
                tsl = slice(tt_ * 128, (tt_ + 1) * 128)
                ps = self.pb[tt_ % 2]
                for kt in range(NDT):
                    k.mmul(ps[:, :], self.hb8[kt][:, tsl], wq[:, kt, :],
                           start=(kt == 0), stop=False)
                k.mmul(ps[:, :], self.ones_col[:], self.brow_bf[0:1, :],
                       start=False, stop=True)
                k.copy("scalar", qslot(tt_)[:, ocs], ps[:, :])
        if _K.TAP == "s3q":
            tap = k.dout("tap2", [2, 128, 2048], BF16)
            k.dma("gpsimd", tap.ap()[b, :, 0:1024], qslot(0))
            k.dma("gpsimd", tap.ap()[b, :, 1024:2048], qslot(8))
            return
        wAf = self.wA[:].rearrange("p a n -> p (a n)")
        wd = wAf[:, 0:512].rearrange("p (a n) -> p a n", a=8)
        k.dma("gpsimd", wd, Wdown.ap().rearrange("(a p) n -> p a n", p=128))
        k.dma("gpsimd", self.brow_bf[0:1, 0:64], bdown_r.ap())
        cT = self.ob8[0]
        for c in range(NCH):
            cs = slice(c * CH, (c + 1) * CH)
            ps = self.pb[2]
            for kt in range(NDT):
                k.mmul(ps[0:64, :], wd[:, kt, :], self.hb8[kt][:, cs],
                       start=(kt == 0), stop=False)
            k.mmul(ps[0:64, :], self.brow_bf[0:1, 0:64],
                   self.ones_row[:], start=False, stop=True)
            k.copy("vector", cT[0:64, cs], ps[0:64, :])
        for c in range(NCH):
            k.dma("gpsimd", self.masks[c][0:64, :],
                  Wup.ap()[:, c * 512:(c + 1) * 512])
        bup_bf = [self.rowm.bitcast(BF16), self.rowv.bitcast(BF16)]
        k.dma("gpsimd", bup_bf[0][0:1, :], bup_r.ap()[:, 0:1024])
        k.dma("gpsimd", bup_bf[1][0:1, :], bup_r.ap()[:, 1024:2048])
        if _K.TAP == "s3c":
            tap = k.dout("tap2", [2, 128, 2048], BF16)
            k.dma("gpsimd", tap.ap()[b, 0:64, :], cT[0:64, :])
            return
        kvb = self.qr[:]
        v2 = self.kr[:, 0:1024].rearrange("p (x g) -> p x g", g=16)
        p3 = self.kr[:, 1024:2048]
        atte = self.rope_s[0]
        for tt_ in range(NTT):
            tsl = slice(tt_ * 128, (tt_ + 1) * 128)
            for c in range(4):
                ps = self.pb[2 + c % 2]
                k.mmul(ps[:, :], cT[0:64, tsl], self.masks[c][0:64, :],
                       start=True, stop=False)
                k.mmul(ps[:, :], self.ones_col[:],
                       bup_bf[c // 2][0:1, (c % 2) * 512:(c % 2 + 1) * 512],
                       start=False, stop=True)
                k.copy("scalar", kvb[:, c * 512:(c + 1) * 512], ps[:, :])
            k.copy("vector", v2,
                   kvb.rearrange("p (g x) -> p g x", g=16)[:, :, 64:128]
                   .transpose([0, 2, 1]))
            qs = qslot(tt_)
            kview = kvb.rearrange("p (g x) -> p g x", g=16)[:, :, 0:64]
            p3g = p3.rearrange("p (g x) -> p g x", g=16)
            for h in range(16):
                qh = qs[:, h * 64:(h + 1) * 64].unsqueeze(1)\
                    .broadcast_to([128, 16, 64])
                k.tt("vector", p3g, qh, kview, ALU.mult)
                k.red(f32[:, h * 16:(h + 1) * 16], p3g, ALU.add)
            k.act(atte[:, 0:256], f32[:, 0:256], AF.Exp, scale=0.125)
            k.red(f32[:, 256:272],
                  atte[:, 0:256].rearrange("p (h g) -> p h g", h=16), ALU.add)
            k.act(f32[:, 352:368], f32[:, 256:272], AF.Ln)
            k.act(f32[:, 272:288], f32[:, 352:368], AF.Exp, scale=-1.0)
            p3v = p3.rearrange("p (x g) -> p x g", g=16)
            for h in range(16):
                ah = atte[:, h * 16:(h + 1) * 16].unsqueeze(1)\
                    .broadcast_to([128, 64, 16])
                k.tt("vector", p3v, ah, v2, ALU.mult)
                k.red(f32[:, 288:352], p3v, ALU.add)
                dst = (self.P_sb[0] if h < 8 else self.P_sb[1])
                k.ts("vector", dst[:, (h % 8) * 64:(h % 8 + 1) * 64],
                     f32[:, 288:352], f32[:, 272 + h:273 + h], ALU.mult)
            if _K.TAP == "s3o2" and tt_ == 0:
                tap = k.dout("tap2", [2, 128, 2048], BF16)
                k.dma("gpsimd", tap.ap()[b, :, 0:512], self.f32s[0][:])
                return
            if _K.TAP == "s3v2" and tt_ == 0:
                tap = k.dout("tap2", [2, 128, 2048], BF16)
                k.dma("gpsimd", tap.ap()[b, :, 0:1024], self.kr[:, 0:1024])
                k.dma("gpsimd", tap.ap()[b, :, 1024:2048], p3)
                return
            if _K.TAP == "s3ot" and tt_ == 0:
                tap = k.dout("tap2", [2, 128, 2048], BF16)
                k.dma("gpsimd", tap.ap()[b, :, 0:512], self.P_sb[0][:])
                k.dma("gpsimd", tap.ap()[b, :, 512:1024], self.P_sb[1][:])
                k.dma("gpsimd", tap.ap()[b, :, 1024:1088].bitcast(F32)[:, 0:32],
                      self.f32s[0][:, 256:288])
                return
            if _K.TAP == "s3kv" and tt_ == 0:
                tap = k.dout("tap2", [2, 128, 2048], BF16)
                k.dma("gpsimd", tap.ap()[b, :, :], kvb)
                return
            if _K.TAP == "s3att" and tt_ == 0:
                tap = k.dout("tap2", [2, 128, 2048], BF16)
                k.dma("gpsimd", tap.ap()[b, :, 0:256], atte[:, 0:256])
                k.dma("gpsimd", tap.ap()[b, :, 256:512].bitcast(F32)[:, 0:128],
                      self.f32s[0][:, 256:384]) if False else None
                return
            for dt in range(NDT):
                src = (self.P_sb[0] if dt < 4 else self.P_sb[1])
                k.trans(self.pbT,
                        src[:, (dt % 4) * 128:(dt % 4 + 1) * 128], self.JT[:])
                k.copy("vector", self.ob8[dt][:, tsl], self.pbT)
        if _K.TAP == "s3ob":
            self.dump8(b, self.ob8, BF16)
            return
        wo = self.wA[:]
        for wh in range(2):
            k.dma("gpsimd", wo, Wo.ap()[:, wh * 512:(wh + 1) * 512]
                  .rearrange("(a p) n -> p a n", p=128))
            for jt2 in range(4):
                jt = wh * 4 + jt2
                for c in range(NCH):
                    cs = slice(c * CH, (c + 1) * CH)
                    ps = self.pb[c % 2]
                    for dt in range(NDT):
                        k.mmul(ps[:, :], wo[:, dt, jt2 * 128:(jt2 + 1) * 128],
                               self.ob8[dt][:, cs],
                               start=(dt == 0), stop=(dt == NDT - 1))
                    k.stt(self.x8[jt][:, cs], ps[:, :],
                          self.pcols[:, 16 + jt:17 + jt],
                          self.x8[jt][:, cs], ALU.add, ALU.add)
        if _K.TAP == "s3x1":
            self.dump8(b, self.x8)
            return
        rms_to_hb(8, 1e-6)
        wgv = wAf[:, 0:64].rearrange("p (a n) -> p a n", a=8)
        k.dma("gpsimd", wgv, Wg.ap().rearrange("(a p) n -> p a n", p=128))
        k.dma("gpsimd", self.brow_bf[0:1, 0:8], bg_r.ap())
        geT = self.kr
        for tt_ in range(NTT):
            tsl = slice(tt_ * 128, (tt_ + 1) * 128)
            ps = self.pb[3]
            for kt in range(NDT):
                k.mmul(ps[:, 0:8], self.hb8[kt][:, tsl], wgv[:, kt, :],
                       start=(kt == 0), stop=False)
            k.mmul(ps[:, 0:8], self.ones_col[:], self.brow_bf[0:1, 0:8],
                   start=False, stop=True)
            k.act(f32[:, 0:8], ps[:, 0:8], AF.Exp)
            k.red(f32[:, 8:9], f32[:, 0:8], ALU.add)
            k.act(f32[:, 10:11], f32[:, 8:9], AF.Ln)
            k.act(f32[:, 9:10], f32[:, 10:11], AF.Exp, scale=-1.0)
            k.ts("vector", self.rope_s[0][:, 0:8], f32[:, 0:8],
                 f32[:, 9:10], ALU.mult)
            k.trans(self.pbT[0:8, :], self.rope_s[0][:, 0:8], self.JT[:])
            k.copy("vector", geT[0:8, tsl], self.pbT[0:8, :])
        zsl = [vp[:, i * 512:(i + 1) * 512] for i in range(16)]
        selbuf = self.selb
        wza = wAf[:, 1024:2048].rearrange("p (a n) -> p a n", a=8)
        gbc = self.rope_c[0]
        for cp in range(2):
            for jp in range(4):
                for e in range(8):
                    k.dma("gpsimd", selbuf[:],
                          k.const("sel8").ap()[:, e * 128:(e + 1) * 128])
                    for ci in range(2):
                        c = cp * 2 + ci
                        cs = slice(c * CH, (c + 1) * CH)
                        k.mmul(self.pb[4][:, :], selbuf[:], geT[0:8, cs])
                        k.copy("vector", gbc[:], self.pb[4][:, :])
                        for dt in range(NDT):
                            k.tt("vector", zsl[ci * 8 + dt],
                                 self.hb8[dt][:, cs], gbc[:], ALU.mult)
                    for ji in range(2):
                        jt = jp * 2 + ji
                        k.dma("gpsimd", wza,
                              We.ap()[e, :, jt * 128:(jt + 1) * 128]
                              .rearrange("(a p) n -> p a n", p=128))
                        for ci in range(2):
                            ps = self.pb[ji * 2 + ci]
                            for dt in range(NDT):
                                k.mmul(ps[:, :], wza[:, dt, :],
                                       zsl[ci * 8 + dt],
                                       start=(e == 0 and dt == 0),
                                       stop=(e == 7 and dt == NDT - 1))
                for ji in range(2):
                    jt = jp * 2 + ji
                    for ci in range(2):
                        c = cp * 2 + ci
                        cs = slice(c * CH, (c + 1) * CH)
                        k.tt("vector", self.x8[jt][:, cs], self.x8[jt][:, cs],
                             self.pb[ji * 2 + ci][:, :], ALU.add)
        self.unpark_add(b)


def _stage_inputs(inputs):
    st = {}
    x = inputs["L0_attn_x"]
    xf = np.ascontiguousarray(np.transpose(x, (0, 2, 1)))
    st["x_fm"] = xf.reshape(NCORE, 2, D, T)
    tr = lambda a: np.ascontiguousarray(np.transpose(a))
    st["L0_attn_WqT"] = tr(inputs["L0_attn_Wq"])
    st["L0_attn_WkT"] = tr(inputs["L0_attn_Wk"])
    st["L0_attn_WvT"] = tr(inputs["L0_attn_Wv"])
    st["L0_attn_WoT"] = tr(inputs["L0_attn_Wo"])
    pcol = lambda v: np.ascontiguousarray(v.reshape(8, 128).T)
    row = lambda v: np.ascontiguousarray(v.reshape(1, -1))
    st["L1_attn_WqT"] = tr(inputs["L1_attn_Wq"])
    st["L1_attn_WkT"] = tr(inputs["L1_attn_Wk"])
    st["L1_attn_WvT"] = tr(inputs["L1_attn_Wv"])
    st["L1_attn_WuT"] = tr(inputs["L1_attn_Wu"])
    st["L1_attn_WoT"] = tr(inputs["L1_attn_Wo"])
    st["L1_attn_bq_p"] = pcol(inputs["L1_attn_bq"])
    st["L1_attn_bk_p"] = pcol(inputs["L1_attn_bk"])
    st["L1_attn_bv_r"] = row(inputs["L1_attn_bv"])
    st["L1_attn_bu_p"] = pcol(inputs["L1_attn_bu"])
    st["L1_attn_bo_p"] = pcol(inputs["L1_attn_bo"])
    st["L1_attn_scale_p"] = pcol(inputs["L1_attn_scale"])
    st["L1_ffn_Wq"] = np.ascontiguousarray(inputs["L1_ffn_Wq"])
    st["L1_ffn_Wdown"] = np.ascontiguousarray(inputs["L1_ffn_Wdown"])
    st["L1_ffn_Wup"] = np.ascontiguousarray(inputs["L1_ffn_Wup"])
    st["L1_ffn_Wo"] = np.ascontiguousarray(inputs["L1_ffn_Wo"])
    st["L1_ffn_Wg"] = np.ascontiguousarray(inputs["L1_ffn_Wg"])
    st["L1_ffn_We"] = np.ascontiguousarray(inputs["L1_ffn_We"])
    st["L1_ffn_bq_r"] = row(inputs["L1_ffn_bq"])
    st["L1_ffn_bdown_r"] = row(inputs["L1_ffn_bdown"])
    st["L1_ffn_bup_r"] = row(inputs["L1_ffn_bup"])
    st["L1_ffn_bg_r"] = row(inputs["L1_ffn_bg"])
    st["L1_ffn_gamma1_p"] = pcol(inputs["L1_ffn_gamma1"])
    st["L1_ffn_gamma2_p"] = pcol(inputs["L1_ffn_gamma2"])
    st["L1_ffn_bo_p"] = pcol(inputs["L1_ffn_bo"])
    st["L0_ffn_wqkv"] = np.ascontiguousarray(inputs["L0_ffn_wqkv"])
    st["L0_ffn_wo"] = np.ascontiguousarray(inputs["L0_ffn_wo"])
    st["L0_ffn_w1"] = np.ascontiguousarray(inputs["L0_ffn_w1"])
    st["L0_ffn_w2"] = np.ascontiguousarray(inputs["L0_ffn_w2"])
    st["L0_ffn_b1_r"] = np.ascontiguousarray(inputs["L0_ffn_b1"])
    st["L0_ffn_qs_p"] = np.ascontiguousarray(
        np.tile(inputs["L0_ffn_q_scale"], (1, 2))[:, :, None])
    st["L0_ffn_ks_p"] = np.ascontiguousarray(
        np.tile(inputs["L0_ffn_k_scale"], (1, 2))[:, :, None])
    pcol2 = lambda v: np.ascontiguousarray(
        v.reshape(2, 8, 128).transpose(0, 2, 1))
    st["L0_ffn_lg_p"] = pcol2(inputs["L0_ffn_ln_g"])
    st["L0_ffn_lb_p"] = pcol2(inputs["L0_ffn_ln_b"])
    st["L0_ffn_b2_p"] = pcol2(inputs["L0_ffn_b2"])
    return st


TAP = None
BLOCKS = ("s0", "s1", "s2", "s3")


def _build_s0_only():
    k = K()
    mk = MK(k)
    mk.load_consts()
    for b in range(2):
        mk.load_x(b)
        if TAP == "ln":
            mk.ln_plain(mk.x8, mk.hb8)
            tap = k.dout("tap", [2, D, T], F32)
            for dt in range(NDT):
                k.dma("gpsimd", tap.ap()[b, dt * 128:(dt + 1) * 128, :],
                      mk.hb8[dt][:])
            mk.store_x(b)
            continue
        if TAP in ("lnv", "lnw", "lnw_nocast", "lnw_sync"):
            mk.ln_plain(mk.x8, mk.hb8)
            WvT = k.din("L0_attn_WvT", [D, D])
            wv = mk.wA[:]
            if TAP == "lnw_nocast":
                wvf = mk.wA[:].bitcast(F32)
                k.dma("gpsimd", wvf[:, 0:2048],
                      WvT.ap()[:, 0:256].rearrange("(a p) n -> p (a n)", p=128))
            elif TAP == "lnw_sync":
                k.dma("sync", wv,
                      WvT.ap()[:, 0:512].rearrange("(a p) n -> p a n", p=128)) if False else None
                wvf = mk.wA[:].bitcast(F32)
                k.dma("sync", wvf[:, 0:2048],
                      WvT.ap()[:, 0:256].rearrange("(a p) n -> p (a n)", p=128))
            else:
                k.dma("gpsimd", wv,
                      WvT.ap()[:, 0:512].rearrange("(a p) n -> p a n", p=128))
            if TAP == "lnv":
                k.memset("vector", mk.vplus[:, :, :, 64:65], 1.0)
                for tt_ in range(NTT):
                    ps = mk.pb[tt_ % 2]
                    tsl = slice(tt_ * 128, (tt_ + 1) * 128)
                    for dt in range(NDT):
                        k.mmul(ps[:, :], mk.hb8[dt][:, tsl], wv[:, dt, :],
                               start=(dt == 0), stop=(dt == NDT - 1))
                    k.copy("vector", mk.vplus[:, :, tt_, 0:64],
                           ps[:].rearrange("p (a n) -> p a n", a=8))
            tap = k.dout("tap", [2, D, T], BF16)
            for dt in range(NDT):
                k.dma("sync", tap.ap()[b, dt * 128:(dt + 1) * 128, :],
                      mk.hb8[dt][:])
            mk.store_x(b)
            continue
        for blk in BLOCKS:
            getattr(mk, blk)(b)
        if TAP == "ob":
            tap = k.dout("tap", [2, D, T], F32)
            for dt in range(NDT):
                k.dma("gpsimd", tap.ap()[b, dt * 128:(dt + 1) * 128, :],
                      mk.ob8[dt][:])
        if TAP == "qr":
            tap = k.dout("tap", [2, 2 * 128, T], F32)
            k.dma("gpsimd", tap.ap()[b, 0:128, :], mk.qr[:])
            k.dma("gpsimd", tap.ap()[b, 128:256, :], mk.kr[:])
        if TAP == "wv":
            tap = k.dout("tap", [2, 128, 8 * 512], F32)
            k.dma("gpsimd", tap.ap()[b, :, :],
                  mk.wA[:].rearrange("p a b -> p (a b)"))
        if TAP == "vplus":
            tap = k.dout("tap", [2, 128, 8 * 16 * 65], F32)
            k.dma("gpsimd", tap.ap()[b, :, :],
                  mk.vplus[:].rearrange("p a b c -> p (a b c)"))
        mk.store_x(b)
    mk.finish()
    return k


def kernel(**inputs):
    import time
    t0 = time.time()
    st = _stage_inputs(inputs)
    k = _build_s0_only()
    nc = k.build()
    print("[kernel] build", time.time() - t0, "ops:",
          {e: len(k.ops[e]) for e in ENGS})
    t0 = time.time()
    per_core = []
    for c in range(NCORE):
        m = {}
        for nm in k.input_names:
            m[nm] = st["x_fm"][c] if nm == "x_fm" else st[nm]
        per_core.append(m)
    res = run_bass_kernel_spmd(nc, per_core, core_ids=list(range(NCORE)))
    print("[kernel] run", time.time() - t0)
    out = np.empty((B, T, D), dtype=np.float32)
    for c in range(NCORE):
        y = res.results[c]["y_fm"]
        for b in range(2):
            out[2 * c + b] = y[b].T
    if TAP is not None:
        kernel.tap = [res.results[c].get("tap") for c in range(NCORE)]
        kernel.tap2 = [res.results[c].get("tap2") for c in range(NCORE)]
    return out
